# version 01
# baseline (speedup 1.0000x reference)
import math
import numpy as np
import ml_dtypes
import concourse.bass as bass
import concourse.mybir as mybir
from concourse.bass_utils import run_bass_kernel_spmd

F32 = mybir.dt.float32
BF16 = mybir.dt.bfloat16
AF = mybir.ActivationFunctionType
ALU = mybir.AluOpType
AX = mybir.AxisListType
BF16NP = ml_dtypes.bfloat16

NCORES = 8
D = 1024
B = 4
T = 4096
NTOK = B * T
TOKC = NTOK // NCORES
DH = 4096
CH = 512



class Prog:

    def __init__(self, nc, es):
        self.nc = nc
        self.es = es
        self.esem = {}
        self.ecnt = {}
        for nm in ("PE", "DVE", "Activation", "Pool", "SP"):
            self.esem[nm] = es.enter_context(nc.semaphore(f"pg_{nm}"))
            self.ecnt[nm] = 0
        self.dma = es.enter_context(nc.semaphore("pg_dma"))
        self.ndma = 0
        self.gdma = es.enter_context(nc.semaphore("pg_gdma"))
        self.ngdma = 0

    def dma_start(self, out, in_, eng=None, **kw):
        eng = eng or self.nc.sync
        if eng is self.nc.gpsimd:
            if self.ngdma:
                eng.wait_ge(self.gdma, 16 * self.ngdma)
            eng.dma_start(out=out, in_=in_, **kw).then_inc(self.gdma, 16)
            self.ngdma += 1
        else:
            if self.ndma:
                eng.wait_ge(self.dma, 16 * self.ndma)
            eng.dma_start(out=out, in_=in_, **kw).then_inc(self.dma, 16)
            self.ndma += 1

    def wait_dma(self, *engines, upto=None):
        for e in engines:
            if self.ndma:
                e.wait_ge(self.dma, 16 * self.ndma)
            if self.ngdma:
                e.wait_ge(self.gdma, 16 * self.ngdma)

    def chain(self, inst, eng):
        self.bar([inst], [eng])

    def sync(self, consumers, engines=("PE", "DVE", "Activation", "Pool", "SP")):
        for c in consumers:
            for nm in engines:
                if self.ecnt[nm]:
                    c.wait_ge(self.esem[nm], self.ecnt[nm])

    def bar(self, producer_insts, consumers):
        touched = set()
        for p in producer_insts:
            nm = str(p.ins.engine)
            nm = nm.split(".")[-1]
            p.then_inc(self.esem[nm], 1)
            self.ecnt[nm] += 1
            touched.add(nm)
        for c in consumers:
            for nm in touched:
                c.wait_ge(self.esem[nm], self.ecnt[nm])


class PsumPipe:

    def __init__(self, nc, es, name, shape, nbanks=2, dtype=F32):
        self.nc = nc
        self.tiles = [es.enter_context(nc.psum_tensor(f"{name}{i}", shape, dtype))
                      for i in range(nbanks)]
        self.sem_mm = es.enter_context(nc.semaphore(f"{name}_mm"))
        self.sem_ev = es.enter_context(nc.semaphore(f"{name}_ev"))
        self.nb = nbanks
        self.i = 0
        self.n_ev = 0

    def acquire(self):
        need = self.i - self.nb + 1
        if need > 0:
            self.nc.tensor.wait_ge(self.sem_ev, need)
        tile = self.tiles[self.i % self.nb]
        self.i += 1
        return tile

    def mm_done(self, inst):
        inst.then_inc(self.sem_mm, 1)
        self.n_mm = getattr(self, "n_mm", 0) + 1

    def evac_wait(self, *engines):
        for eng in engines:
            eng.wait_ge(self.sem_mm, self.n_ev + 1)
        return self.tiles[self.n_ev % self.nb]

    def evac_done(self, inst):
        inst.then_inc(self.sem_ev, 1)
        self.n_ev += 1

    def drain(self, *engines):
        for e in engines:
            e.wait_ge(self.sem_ev, self.i)

    def guard(self, *engines):
        for e in engines:
            e.wait_ge(self.sem_ev, self.n_ev)


class Loader:

    def __init__(self, nc, es, name, shape, dtype=BF16, nbuf=2):
        self.nc = nc
        self.bufs = [es.enter_context(nc.sbuf_tensor(f"{name}{i}", shape, dtype))
                     for i in range(nbuf)]
        self.sems = [es.enter_context(nc.semaphore(f"{name}_ld{i}"))
                     for i in range(nbuf)]
        self.nb = nbuf
        self.n_issued = 0
        self.n_used = 0

    def issue(self, in_ap, eng=None, wait=None):
        eng = eng or self.nc.gpsimd
        if wait is not None:
            eng.wait_ge(wait[0], wait[1])
        i = self.n_issued % self.nb
        eng.dma_start(out=self.bufs[i][:], in_=in_ap).then_inc(self.sems[i], 16)
        self.n_issued += 1

    def use(self, *engines):
        i = self.n_used % self.nb
        nloads = self.n_used // self.nb + 1
        for eng in engines:
            eng.wait_ge(self.sems[i], 16 * nloads)
        buf = self.bufs[i]
        self.n_used += 1
        return buf



def prep_butterfly_arrays(w):
    nstage, dim = w.shape[0], 2 * w.shape[1]
    n = np.arange(dim)
    wdiag = np.empty((nstage, dim), np.float32)
    woffn = np.empty((nstage, dim), np.float32)
    for s in range(nstage):
        S = 1 << s
        q = n // (2 * S)
        rem = n % (2 * S)
        bitset = rem >= S
        r = np.where(bitset, rem - S, rem)
        ws = w[s].reshape(dim // (2 * S), S, 2, 2)
        wdiag[s] = np.where(bitset, ws[q, r, 1, 1], ws[q, r, 0, 0])
        woffn[s] = np.where(bitset, ws[q, r, 1, 0], ws[q, r, 0, 1])
    f = np.arange(dim)
    fprime_of_f = (f % 128) * 32 + (f // 128)
    f_of_fprime = np.empty(dim, np.int64)
    f_of_fprime[fprime_of_f] = f
    return {
        "p1_diag": wdiag[:7].copy(), "p1_offn": woffn[:7].copy(),
        "p2_diag": wdiag[7:12][:, f_of_fprime].copy(),
        "p2_offn": woffn[7:12][:, f_of_fprime].copy(),
        "fprime_of_f": fprime_of_f, "f_of_fprime": f_of_fprime,
    }


def colmajor(vec, p=128):
    return np.ascontiguousarray(vec.reshape(-1, p).T)


def pairmask(dist):
    m = np.zeros((128, 128), np.float32)
    i = np.arange(128)
    m[i, i ^ dist] = 1.0
    return m



class Ctx:
    def __init__(self, nc, es, pr):
        self.nc, self.es, self.pr = nc, es, pr
        self.v, self.s, self.g, self.t, self.sy = (
            nc.vector, nc.scalar, nc.gpsimd, nc.tensor, nc.sync)
        self.ident_bf = es.enter_context(nc.sbuf_tensor("c_identb", [128, 128], BF16))
        self.ones_bf = es.enter_context(nc.sbuf_tensor("c_onesb", [128, 128], BF16))
        self.ones_f = es.enter_context(nc.sbuf_tensor("c_onesf", [128, 128], F32))
        idb = nc.inline_tensor(np.eye(128, dtype=BF16NP), "c_identb_d")
        pr.dma_start(self.ident_bf[:], idb.ap())
        self.init_insts = [self.v.memset(self.ones_bf[:], 1.0),
                           self.v.memset(self.ones_f[:], 1.0)]

    def finish_init(self):
        cons = [self.v, self.s, self.g, self.t, self.sy]
        self.pr.wait_dma(*cons)
        self.pr.bar(self.init_insts, cons)
        self.init_insts = []


def build_ln(cx, x_tiles, xh_tiles, sq, row, bc, pbank):
    v, s, t, pr = cx.v, cx.s, cx.t, cx.pr
    nk = len(x_tiles)
    mu = row[0:1, 0:CH]
    rs = row[0:1, CH:2 * CH]
    pr.sync([s], engines=("DVE",))
    mm = None
    for k in range(nk):
        isq = s.activation(out=sq[:], in_=x_tiles[k][:], func=AF.Square)
        pr.bar([isq], [t])
        t.matmul(pbank[0:1, :], cx.ones_f[:, 0:1], x_tiles[k][:],
                 start=(k == 0), stop=(k == nk - 1))
        mm = t.matmul(pbank[32:33, :], cx.ones_f[:, 0:1], sq[:],
                      start=(k == 0), stop=(k == nk - 1))
        if k < nk - 1:
            pr.bar([mm], [s])
    pr.bar([mm], [v])
    i1 = v.tensor_scalar_mul(mu, pbank[0:1, :], 1.0 / D)
    i2 = v.tensor_scalar_mul(rs, pbank[32:33, :], 1.0 / D)
    pr.bar([i1, i2], [v, t])
    ia = v.tensor_mul(sq[0:1, :], mu, mu)
    pr.chain(ia, v)
    ib_ = v.tensor_sub(rs, rs, sq[0:1, :])
    pr.chain(ib_, v)
    i3 = v.tensor_scalar_add(rs, rs, 1e-5)
    pr.bar([i3], [s])
    i4 = s.activation(out=rs, in_=rs, func=AF.Ln)
    pr.bar([i4], [v])
    i4b = v.tensor_scalar_mul(rs, rs, -0.5)
    pr.bar([i4b], [s])
    i5 = s.activation(out=rs, in_=rs, func=AF.Exp)
    pr.bar([i5], [t, v])
    mmu = t.matmul(pbank[:, :], cx.ones_f[0:1, :], mu, start=True, stop=True)
    pr.bar([mmu], [v])
    icp = v.tensor_copy(out=bc[:], in_=pbank[:, :])
    pr.bar([icp], [t])
    mrs = t.matmul(pbank[:, :], cx.ones_f[0:1, :], rs, start=True, stop=True)
    pr.bar([mrs], [v])
    last = None
    for k in range(nk):
        iu = v.tensor_sub(sq[:], x_tiles[k][:], bc[:])
        pr.chain(iu, v)
        last = v.tensor_mul(xh_tiles[k][:], sq[:], pbank[:, :])
        pr.chain(last, v)
    return last



FFN0_INPUTS = [
    ("xT", [D, TOKC], F32),
    ("wexp", [D, DH], F32),
    ("wconp", [DH, D], F32),
    ("bexp_cm", [128, 32], F32), ("ubias_cm", [128, 32], F32),
    ("bcon_cm", [128, 8], F32), ("dbias_cm", [128, 8], F32),
    ("up_d1", [128, 7, 32], F32), ("up_o1", [128, 7, 32], F32), ("up_o1f", [128, 7, 32], F32),
    ("up_d2", [128, 5, 32], F32), ("up_o2", [128, 5, 32], F32), ("up_o2f", [128, 5, 32], F32),
    ("dn_d1", [128, 7, 32], F32), ("dn_o1", [128, 7, 32], F32), ("dn_o1f", [128, 7, 32], F32),
    ("dn_d2", [128, 5, 32], F32), ("dn_o2", [128, 5, 32], F32), ("dn_o2f", [128, 5, 32], F32),
]


def build_ffn_l0(nc):
    from contextlib import ExitStack
    ap_in = {}
    for name, shape, dt in FFN0_INPUTS:
        ap_in[name] = nc.dram_tensor(name, shape, dt, kind="ExternalInput").ap()
    xTo = nc.dram_tensor("xTo", [D, TOKC], F32, kind="ExternalOutput").ap()
    scr = nc.dram_tensor("scr_perm", [DH, CH], BF16, kind="Internal").ap()
    with ExitStack() as es:
        pr = Prog(nc, es)
        cx = Ctx(nc, es, pr)
        emit_ffn_l0(cx, es, ap_in, ap_in["xT"], xTo, scr, prefix="f0")
        pr.wait_dma(cx.sy)
    return nc


def emit_ffn_l0(cx, es, ap_in, xT, xTo, scr, prefix):
    nc, pr, v, s, g, t, sy = cx.nc, cx.pr, cx.v, cx.s, cx.g, cx.t, cx.sy
    sb = lambda name, shape, dt=F32: es.enter_context(
        nc.sbuf_tensor(prefix + name, shape, dt))

    NCHUNK = TOKC // CH
    x_t = [sb(f"x{k}", [128, TOKC]) for k in range(8)]
    xh_t = [sb(f"xh{k}", [128, CH], BF16) for k in range(8)]
    h = sb("h", [128, 32 * CH], BF16)
    sq = sb("sq", [128, CH])
    row = sb("row", [1, 2 * CH])
    bc = sb("bc", [128, CH])
    LT = {ph: sb(f"LT{ph}", [128, 32 * 128], BF16) for ph in ("u1", "u2", "d1", "d2")}
    ident_f = sb("identf", [128, 128])
    lbuild = sb("lbuild", [128, 128])
    lbuild2 = sb("lbuild2", [128, 128])
    rbuild = sb("rbuild", [128, 128])
    cf = {}
    for bname in ("up", "dn"):
        cf[bname + "_d1"] = sb("c" + bname + "d1", [128, 7 * 32])
        cf[bname + "_o1"] = sb("c" + bname + "o1", [128, 7 * 32])
        cf[bname + "_o1f"] = sb("c" + bname + "o1f", [128, 7 * 32])
        cf[bname + "_d2"] = sb("c" + bname + "d2", [128, 5 * 32])
        cf[bname + "_o2"] = sb("c" + bname + "o2", [128, 5 * 32])
        cf[bname + "_o2f"] = sb("c" + bname + "o2f", [128, 5 * 32])
    bias_exp = sb("biasexp", [128, 32])
    bias_ub = sb("biasub", [128, 32])
    bias_out = sb("biasout", [128, 8])
    dtmp = sb("dtmp", [128, 8])
    masks = sb("masks", [128, 128 * 7])

    pp = PsumPipe(nc, es, prefix + "pp", [128, CH], nbanks=4)
    pcomp = es.enter_context(nc.psum_tensor(prefix + "pcomp", [128, 128], F32))
    pln = es.enter_context(nc.psum_tensor(prefix + "pln", [128, CH], F32))
    wload = Loader(nc, es, prefix + "wl", [128, 8 * 128], BF16, nbuf=3)
    cload = Loader(nc, es, prefix + "cl", [128, 32 * 128], BF16, nbuf=2)

    mask_d = nc.inline_tensor(
        np.concatenate([pairmask(1 << si) for si in range(7)], axis=1).astype(np.float32),
        prefix + "c_masks")
    idf_d = nc.inline_tensor(np.eye(128, dtype=np.float32), prefix + "c_idf")
    pr.dma_start(masks[:], mask_d.ap())
    pr.dma_start(ident_f[:], idf_d.ap())
    pr.dma_start(bias_exp[:], ap_in["bexp_cm"])
    pr.dma_start(bias_ub[:], ap_in["ubias_cm"])
    pr.dma_start(bias_out[:], ap_in["bcon_cm"])
    pr.dma_start(dtmp[:], ap_in["dbias_cm"])
    for key in cf:
        pr.dma_start(cf[key][:], ap_in[key])
    for k in range(8):
        pr.dma_start(x_t[k][:], xT[k * 128:(k + 1) * 128, :])
    cx.finish_init()
    ibo = v.tensor_add(bias_out[:], bias_out[:], dtmp[:])
    pr.chain(ibo, v)
    ib = v.tensor_scalar_mul(bias_out[:], bias_out[:], 0.5)
    pr.bar([ib], [v, s])

    def compose(ph, dkey, okey, ofkey, nstage, mask_of_stage):
        for grp in range(32):
            gg = slice(grp * 128, (grp + 1) * 128)
            for idx, si in enumerate(range(nstage - 1, -1, -1)):
                dcol = cf[dkey][:, si * 32 + grp: si * 32 + grp + 1]
                ocol = cf[okey][:, si * 32 + grp: si * 32 + grp + 1]
                ofcol = cf[ofkey][:, si * 32 + grp: si * 32 + grp + 1]
                mcol = masks[:, mask_of_stage(si) * 128:(mask_of_stage(si) + 1) * 128]
                if idx == 0:
                    v.tensor_scalar_mul(rbuild[:], ident_f[:], dcol)
                    ic2 = v.tensor_scalar_mul(lbuild2[:], mcol, ofcol)
                    pr.chain(ic2, v)
                    iv = v.tensor_add(rbuild[:], rbuild[:], lbuild2[:])
                else:
                    v.tensor_scalar_mul(lbuild[:], ident_f[:], dcol)
                    ic2 = v.tensor_scalar_mul(lbuild2[:], mcol, ocol)
                    pr.chain(ic2, v)
                    iv = v.tensor_add(lbuild[:], lbuild[:], lbuild2[:])
                    pr.bar([iv], [t])
                    im = t.matmul(pcomp[:, :], lbuild[:], rbuild[:], start=True, stop=True)
                    pr.bar([im], [v])
                    iv = v.tensor_copy(out=rbuild[:], in_=pcomp[:, :])
                pr.bar([iv], [v, t])
            ic = v.tensor_copy(out=LT[ph][:, gg], in_=rbuild[:])
            pr.bar([ic], [v, t])

    compose("u1", "up_d1", "up_o1", "up_o1f", 7, lambda si: si)
    compose("u2", "up_d2", "up_o2", "up_o2f", 5, lambda si: si)
    compose("d1", "dn_d1", "dn_o1", "dn_o1f", 7, lambda si: si)
    compose("d2", "dn_d2", "dn_o2", "dn_o2f", 5, lambda si: si)

    wexp_r = ap_in["wexp"].rearrange("(k p) c -> p k c", p=128)
    wcon_r = ap_in["wconp"].rearrange("(k p) c -> p k c", p=128)
    hv = h.rearrange("p (j c) -> p j c", j=32)
    hv4 = [h[m * 32:(m + 1) * 32, :].rearrange("p (g c) -> p g c", g=32)
           for m in range(4)]
    scr_f = scr.rearrange("(j r) t -> r j t", j=32)
    scr_fp = scr.rearrange("(j g m) t -> m j g t", j=32, g=32)

    def phase_mm(ph, gelu=False):
        last = None
        for grp in range(32):
            gg = slice(grp * 128, (grp + 1) * 128)
            cc = slice(grp * CH, (grp + 1) * CH)
            psum = pp.acquire()
            im = t.matmul(psum[:, :], LT[ph][:, gg], h[:, cc], start=True, stop=True)
            pp.mm_done(im)
            pt = pp.evac_wait(s)
            if gelu:
                ie = s.activation(out=h[:, cc], in_=pt[:, :], func=AF.Gelu,
                                  bias=bias_ub[:, grp:grp + 1])
            else:
                ie = s.activation(out=h[:, cc], in_=pt[:, :], func=AF.Copy)
            pp.evac_done(ie)
            last = ie
        pp.drain(t, v, s, g, sy)

    def permute(to_fprime):
        if to_fprime:
            pr.dma_start(scr_f, hv)
            pr.wait_dma(sy, g)
            for m in range(4):
                pr.dma_start(hv4[m], scr_fp[m], eng=g)
        else:
            for m in range(4):
                pr.dma_start(scr_fp[m], hv4[m])
            pr.wait_dma(sy, g)
            pr.dma_start(hv, scr_f, eng=g)
        pr.wait_dma(t, v, s, g, sy)

    for c in range(NCHUNK):
        cols = slice(c * CH, (c + 1) * CH)
        xs = [xt[:, cols] for xt in x_t]
        build_ln(cx, xs, [xx[:] for xx in xh_t], sq, row, bc, pln)
        pr.sync([t, g], engines=("DVE",))
        mmcnt = {}
        wload.issue(wexp_r[:, :, 0:128])
        wload.issue(wexp_r[:, :, 128:256])
        wload.issue(wexp_r[:, :, 256:384])
        for m in range(32):
            wb = wload.use(t)
            psum = pp.acquire()
            im = None
            for k in range(8):
                im = t.matmul(psum[:, :], wb[:, k * 128:(k + 1) * 128], xh_t[k][:],
                              start=(k == 0), stop=(k == 7))
            pp.mm_done(im)
            mmcnt[m] = pp.n_mm
            if m + 3 < 32:
                wload.issue(wexp_r[:, :, (m + 3) * 128:(m + 4) * 128],
                            wait=(pp.sem_mm, mmcnt[m]))
            pt = pp.evac_wait(s)
            ie = s.activation(out=h[:, m * CH:(m + 1) * CH], in_=pt[:, :],
                              func=AF.Identity, bias=bias_exp[:, m:m + 1])
            pp.evac_done(ie)
        pp.drain(t, v, g, sy)
        phase_mm("u1")
        permute(True)
        phase_mm("u2", gelu=True)
        permute(False)
        phase_mm("d1")
        permute(True)
        phase_mm("d2")
        cload.issue(wcon_r[:, :, 0:128])
        cload.issue(wcon_r[:, :, 128:256])
        for mo in range(8):
            wb = cload.use(t)
            psum = pp.acquire()
            im = None
            for kk in range(32):
                im = t.matmul(psum[:, :], wb[:, kk * 128:(kk + 1) * 128],
                              h[:, kk * CH:(kk + 1) * CH],
                              start=(kk == 0), stop=(kk == 31))
            pp.mm_done(im)
            mmcnt[("c", mo)] = pp.n_mm
            if mo + 2 < 8:
                cload.issue(wcon_r[:, :, (mo + 2) * 128:(mo + 3) * 128],
                            wait=(pp.sem_mm, mmcnt[("c", mo)]))
            pt = pp.evac_wait(v)
            xsl = x_t[mo][:, cols]
            ixa = v.tensor_scalar_add(xsl, xsl, bias_out[:, mo:mo + 1])
            pr.chain(ixa, v)
            ie = v.scalar_tensor_tensor(out=xsl, in0=pt[:, :], scalar=0.5, in1=xsl,
                                        op0=ALU.mult, op1=ALU.add)
            pp.evac_done(ie)
        pp.drain(t, s, g, sy)

    for k in range(8):
        pr.dma_start(xTo[k * 128:(k + 1) * 128, :], x_t[k][:])
    pr.wait_dma(sy)


def host_prep_ffn_l0(inputs, which):
    W_exp = np.asarray(inputs[f"{which}_ffn_W_exp"], np.float32)
    W_con = np.asarray(inputs[f"{which}_ffn_W_con"], np.float32)
    b_exp = np.asarray(inputs[f"{which}_ffn_b_exp"], np.float32)
    up_b = np.asarray(inputs[f"{which}_ffn_up_bias"], np.float32)
    b_con = np.asarray(inputs[f"{which}_ffn_b_con"], np.float32)
    d_bias = np.asarray(inputs[f"{which}_ffn_down_bias"], np.float32)
    up_w = np.asarray(inputs[f"{which}_ffn_up_weights"], np.float32)
    dn_w = np.asarray(inputs[f"{which}_ffn_down_weights"], np.float32)
    bu = prep_butterfly_arrays(up_w)
    bd = prep_butterfly_arrays(dn_w)
    fp = bu["fprime_of_f"]
    d = {
        "wexp": W_exp,
        "wconp": np.ascontiguousarray(W_con[bd["f_of_fprime"], :]),
        "bexp_cm": colmajor(b_exp),
        "ubias_cm": colmajor(up_b[bu["f_of_fprime"]]),
        "bcon_cm": colmajor(b_con),
        "dbias_cm": colmajor(d_bias),
    }

    def stage_cm(arr):
        ns = arr.shape[0]
        out = np.empty((128, ns, 32), np.float32)
        for si in range(ns):
            out[:, si, :] = colmajor(arr[si])
        return out

    flip1 = [np.arange(4096) ^ (1 << si) for si in range(7)]
    flip2 = [np.arange(4096) ^ (1 << si) for si in range(5)]
    for pref, bb in (("up", bu), ("dn", bd)):
        d[pref + "_d1"] = stage_cm(bb["p1_diag"])
        d[pref + "_o1"] = stage_cm(bb["p1_offn"])
        d[pref + "_o1f"] = stage_cm(
            np.stack([bb["p1_offn"][si][flip1[si]] for si in range(7)]))
        d[pref + "_d2"] = stage_cm(bb["p2_diag"])
        d[pref + "_o2"] = stage_cm(bb["p2_offn"])
        d[pref + "_o2f"] = stage_cm(
            np.stack([bb["p2_offn"][si][flip2[si]] for si in range(5)]))
    return d


ATT_HEADS = 16
HD = 64
ALIBI_SLOPES = [2.0 ** (-(h / (ATT_HEADS - 1) * 8.0)) for h in range(ATT_HEADS)]


def build_attn0_pre(nc):
    from contextlib import ExitStack
    ap = {}
    for name, shape, dt in [("xT", [D, TOKC], F32), ("wq", [D, D], F32),
                            ("wkv", [D, 512], F32), ("bq_cm", [128, 8], F32),
                            ("bkv_cm", [128, 4], F32), ("qn_cm", [128, 8], F32),
                            ("kn_cm", [128, 2], F32)]:
        ap[name] = nc.dram_tensor(name, shape, dt, kind="ExternalInput").ap()
    qh_o = nc.dram_tensor("qh", [D, TOKC], BF16, kind="ExternalOutput").ap()
    kh_o = nc.dram_tensor("kh", [256, TOKC], BF16, kind="ExternalOutput").ap()
    vh_o = nc.dram_tensor("vh", [256, TOKC], BF16, kind="ExternalOutput").ap()

    with ExitStack() as es:
        pr = Prog(nc, es)
        cx = Ctx(nc, es, pr)
        v, s, g, t, sy = cx.v, cx.s, cx.g, cx.t, cx.sy
        sb = lambda n, sh, dt=F32: es.enter_context(nc.sbuf_tensor("ap" + n, sh, dt))
        x_t = [sb(f"x{k}", [128, TOKC]) for k in range(8)]
        xh_t = [sb(f"xh{k}", [128, CH], BF16) for k in range(8)]
        sq = sb("sq", [128, CH])
        row = sb("row", [1, 2 * CH])
        bc = sb("bc", [128, CH])
        qf = sb("qf", [128, CH])
        qb = [sb(f"qb{k}", [128, TOKC], BF16) for k in range(8)]
        kvb = [sb(f"kvb{k}", [128, TOKC], BF16) for k in range(4)]
        bq = sb("bq", [128, 8])
        bkv = sb("bkv", [128, 4])
        qn = sb("qn", [128, 8])
        kn = sb("kn", [128, 2])
        hsel = sb("hsel", [128, 2])
        hexp = sb("hexp", [2, 128])
        rr = sb("rr", [2, 2 * CH])
        pln = es.enter_context(nc.psum_tensor("appln", [128, CH], F32))
        pp = PsumPipe(nc, es, "appp", [128, CH], nbanks=4)
        wloadq = Loader(nc, es, "apwq", [128, 8 * 128], BF16, nbuf=3)

        hsel_np = np.zeros((128, 2), np.float32)
        hsel_np[0:64, 0] = 1.0
        hsel_np[64:128, 1] = 1.0
        hsel_d = nc.inline_tensor(hsel_np, "ap_hsel")
        hexp_d = nc.inline_tensor(np.ascontiguousarray(hsel_np.T).astype(np.float32), "ap_hexp")
        pr.dma_start(hsel[:], hsel_d.ap())
        pr.dma_start(hexp[:], hexp_d.ap())
        pr.dma_start(bq[:], ap["bq_cm"])
        pr.dma_start(bkv[:], ap["bkv_cm"])
        pr.dma_start(qn[:], ap["qn_cm"])
        pr.dma_start(kn[:], ap["kn_cm"])
        for k in range(8):
            pr.dma_start(x_t[k][:], ap["xT"][k * 128:(k + 1) * 128, :])
        cx.finish_init()
        iqn = v.tensor_scalar_mul(qn[:], qn[:], 0.125)
        pr.bar([iqn], [v, s, t])

        wq_r = ap["wq"].rearrange("(k p) c -> p k c", p=128)
        wkv_r = ap["wkv"].rearrange("(k p) c -> p k c", p=128)

        for c in range(TOKC // CH):
            cols = slice(c * CH, (c + 1) * CH)
            build_ln(cx, [xt[:, cols] for xt in x_t], [xx[:] for xx in xh_t],
                     sq, row, bc, pln)
            pr.sync([t, g], engines=("DVE",))
            def w_ap(m):
                return (wq_r[:, :, m * 128:(m + 1) * 128] if m < 8 else
                        wkv_r[:, :, (m - 8) * 128:(m - 7) * 128])
            mmrec = {}
            for m in range(3):
                wloadq.issue(w_ap(m))
            for m in range(12):
                wb = wloadq.use(t)
                psum = pp.acquire()
                for k in range(8):
                    im = t.matmul(psum[:, :], wb[:, k * 128:(k + 1) * 128],
                                  xh_t[k][:], start=(k == 0), stop=(k == 7))
                pp.mm_done(im)
                mmrec[m] = pp.n_mm
                if m + 3 < 12:
                    wloadq.issue(w_ap(m + 3), wait=(pp.sem_mm, mmrec[m]))
                pt = pp.evac_wait(v, s)
                biases = bq[:, m:m + 1] if m < 8 else bkv[:, m - 8:m - 7]
                iq = s.activation(out=qf[:], in_=pt[:, :], func=AF.Identity, bias=biases)
                pr.bar([iq], [v, s, t])
                is_v = (m >= 8) and (m - 8) >= 2
                if not is_v:
                    i1 = s.activation(out=sq[:], in_=qf[:], func=AF.Square)
                    pr.bar([i1], [t])
                    mmq = t.matmul(pln[0:2, :], hsel[:, 0:2], sq[:], start=True, stop=True)
                    pr.bar([mmq], [v])
                    i2 = v.tensor_scalar(out=rr[0:2, 0:CH], in0=pln[0:2, :],
                                         scalar1=1.0 / HD, scalar2=1e-8,
                                         op0=ALU.mult, op1=ALU.add)
                    pr.bar([i2], [s])
                    i3 = s.activation(out=rr[0:2, 0:CH], in_=rr[0:2, 0:CH], func=AF.Ln)
                    pr.bar([i3], [v])
                    i4 = v.tensor_scalar_mul(rr[0:2, 0:CH], rr[0:2, 0:CH], -0.5)
                    pr.bar([i4], [s])
                    i5 = s.activation(out=rr[0:2, 0:CH], in_=rr[0:2, 0:CH], func=AF.Exp)
                    pr.bar([i5], [t])
                    mmb = t.matmul(pln[:, :], hexp[0:2, :], rr[0:2, 0:CH],
                                   start=True, stop=True)
                    pr.bar([mmb], [v])
                    iv1 = v.tensor_mul(qf[:], qf[:], pln[:, :])
                    pr.chain(iv1, v)
                    gain = qn[:, m:m + 1] if m < 8 else kn[:, m - 8:m - 7]
                    dst = qb[m][:, cols] if m < 8 else kvb[m - 8][:, cols]
                    ie = v.tensor_scalar(out=dst, in0=qf[:], scalar1=gain,
                                         scalar2=None, op0=ALU.mult)
                    pp.evac_done(ie)
                    pp.guard(t, s)
                else:
                    dst = kvb[m - 8][:, cols]
                    ie = v.tensor_copy(out=dst, in_=qf[:])
                    pp.evac_done(ie)
                    pp.guard(t, s)
        pr.sync([sy])
        for k in range(8):
            pr.dma_start(qh_o[k * 128:(k + 1) * 128, :], qb[k][:])
        for k in range(2):
            pr.dma_start(kh_o[k * 128:(k + 1) * 128, :], kvb[k][:])
            pr.dma_start(vh_o[k * 128:(k + 1) * 128, :], kvb[k + 2][:])
        pr.wait_dma(sy)
    return nc

def build_attn0_core(nc, hh_heads=8):
    from contextlib import ExitStack
    ap = {}
    for name, shape, dt in [("qh", [512, T], BF16), ("kh", [128, T], BF16),
                            ("vh", [128, T], BF16), ("ramps", [8, T], F32),
                            ("basecol", [128, 8 * 32], F32),
                            ("dmask", [128, 4 * 512], F32)]:
        ap[name] = nc.dram_tensor(name, shape, dt, kind="ExternalInput").ap()
    yT_o = nc.dram_tensor("yT", [512, T], BF16, kind="ExternalOutput").ap()

    with ExitStack() as es:
        pr = Prog(nc, es)
        cx = Ctx(nc, es, pr)
        v, s, g, t, sy = cx.v, cx.s, cx.g, cx.t, cx.sy
        sb = lambda n, sh, dt=F32: es.enter_context(nc.sbuf_tensor("ac" + n, sh, dt))
        q_t = [sb(f"q{k}", [128, T], BF16) for k in range(4)]
        kh_t = sb("kh", [128, T], BF16)
        vh_t = sb("vh", [128, T], BF16)
        vT = sb("vT", [128, 32 * 128], BF16)
        yt_t = [sb(f"y{k}", [128, T], BF16) for k in range(4)]
        S = sb("S", [128, T])
        W = sb("W", [128, T])
        Bs = sb("Bs", [128, T])
        WM = sb("WM", [128, T], BF16)
        wmt = sb("wmt", [128, 128], BF16)
        maxv = sb("maxv", [128, 64])
        mrow = sb("mrow", [128, 2])
        den = sb("den", [128, 2])
        ramps = sb("ramps", [8, T])
        selh = sb("selh", [8, 8 * 128])
        basecol = sb("basecol", [128, 8 * 32])
        dmask = sb("dmask", [128, 4 * 512])

        ppS = PsumPipe(nc, es, "acS", [128, CH], nbanks=3)
        ptr = es.enter_context(nc.psum_tensor("acptr", [128, 128], BF16))
        py = es.enter_context(nc.psum_tensor("acpy", [64, 128], F32))

        for k in range(4):
            pr.dma_start(q_t[k][:], ap["qh"][k * 128:(k + 1) * 128, :])
        pr.dma_start(kh_t[:], ap["kh"])
        pr.dma_start(vh_t[:], ap["vh"])
        pr.dma_start(ramps[:], ap["ramps"])
        selh_np = np.zeros((8, 8 * 128), np.float32)
        for hh_ in range(8):
            selh_np[hh_, hh_ * 128:(hh_ + 1) * 128] = 1.0
        selh_d = nc.inline_tensor(selh_np, "ac_selh")
        pr.dma_start(selh[:], selh_d.ap())
        pr.dma_start(basecol[:], ap["basecol"])
        pr.dma_start(dmask[:], ap["dmask"])
        cx.finish_init()

        for jt in range(32):
            imt = t.transpose(ptr[:], vh_t[:, jt * 128:(jt + 1) * 128], cx.ident_bf[:])
            pr.bar([imt], [v])
            iev = v.tensor_copy(out=vT[:, jt * 128:(jt + 1) * 128], in_=ptr[:])
            pr.bar([iev], [t])

        qcur = sb("qcur", [64, T], BF16)
        kcur = sb("kcur", [64, T], BF16)
        for h in range(8):
            qrows = slice((h % 2) * 64, (h % 2) * 64 + 64)
            qtile = q_t[h // 2]
            g_loc = h // 4
            pr.sync([sy], engines=("PE", "DVE", "Activation"))
            pr.dma_start(qcur[:], qtile[qrows, :])
            pr.dma_start(kcur[:], kh_t[g_loc * 64:(g_loc + 1) * 64, :])
            pr.wait_dma(t, v, s)
            for i in range(32):
                nj = i // 4 + 1
                width = nj * CH
                for jc in range(nj):
                    jsl = slice(jc * CH, (jc + 1) * CH)
                    psum = ppS.acquire()
                    t.matmul(psum[:, :], qcur[:, i * 128:(i + 1) * 128],
                             kcur[:, jsl], start=True, stop=False)
                    im = t.matmul(psum[:, :], selh[:, h * 128:(h + 1) * 128],
                                  ramps[:, jsl], start=False, stop=True)
                    ppS.mm_done(im)
                    pt = ppS.evac_wait(s)
                    ie = s.activation(out=S[:, jsl], in_=pt[:, :], func=AF.Identity,
                                      bias=basecol[:, h * 32 + i: h * 32 + i + 1])
                    ppS.evac_done(ie)
                ppS.drain(v, s)
                dsl = slice((nj - 1) * CH, nj * CH)
                iv = v.tensor_add(S[:, dsl], S[:, dsl],
                                  dmask[:, (i % 4) * CH:(i % 4 + 1) * CH])
                pr.chain(iv, v)
                iv = v.reduce_max(mrow[:, 0:1], S[:, 0:width], axis=AX.X)
                pr.chain(iv, v)
                iv = v.tensor_scalar_mul(mrow[:, 1:2], mrow[:, 0:1], -1.0)
                pr.bar([iv], [s])
                ie = s.activation(out=W[:, 0:width], in_=S[:, 0:width], func=AF.Exp,
                                  bias=mrow[:, 1:2])
                pr.bar([ie], [v])
                bufs = [W] + [S if r % 2 == 0 else Bs for r in range(8)]
                for r in range(8):
                    iv = v.max(out=maxv[:, r * 8:(r + 1) * 8], in_=bufs[r][:, 0:width])
                    pr.chain(iv, v)
                    iv = v.match_replace(out=bufs[r + 1][:, 0:width],
                                         in_to_replace=maxv[:, r * 8:(r + 1) * 8],
                                         in_values=bufs[r][:, 0:width], imm_value=0.0)
                    pr.chain(iv, v)
                iv = v.scalar_tensor_tensor(out=WM[:, 0:width], in0=W[:, 0:width],
                                            scalar=maxv[:, 63:64], in1=W[:, 0:width],
                                            op0=ALU.is_ge, op1=ALU.mult)
                pr.chain(iv, v)
                iv = v.reduce_sum(den[:, 0:1], WM[:, 0:width], axis=AX.X)
                pr.chain(iv, v)
                iv = v.reciprocal(den[:, 1:2], den[:, 0:1])
                pr.chain(iv, v)
                iv = v.tensor_scalar(out=WM[:, 0:width], in0=WM[:, 0:width],
                                     scalar1=den[:, 1:2], scalar2=None, op0=ALU.mult)
                pr.bar([iv], [t])
                imav = None
                for jt in range(nj * 4):
                    imt = t.transpose(ptr[:], WM[:, jt * 128:(jt + 1) * 128],
                                      cx.ident_bf[:])
                    pr.bar([imt], [v])
                    iev = v.tensor_copy(out=wmt[:], in_=ptr[:])
                    pr.bar([iev], [t])
                    imav = t.matmul(py[:, :], vT[:, jt * 128 + g_loc * 64:
                                                 jt * 128 + g_loc * 64 + 64],
                                    wmt[:], start=(jt == 0), stop=(jt == nj * 4 - 1))
                pr.bar([imav], [v])
                iev = v.tensor_copy(
                    out=yt_t[h // 2][(h % 2) * 64:(h % 2) * 64 + 64,
                                     i * 128:(i + 1) * 128],
                    in_=py[:, :])
                pr.bar([iev], [t, s])
        pr.sync([sy])
        for k in range(4):
            pr.dma_start(yT_o[k * 128:(k + 1) * 128, :], yt_t[k][:])
        pr.wait_dma(sy)
    return nc

def build_attn0_post(nc):
    from contextlib import ExitStack
    ap = {}
    for name, shape, dt in [("xT", [D, TOKC], F32), ("yTs", [D, TOKC], BF16),
                            ("wo", [D, D], F32), ("bo_cm", [128, 8], F32)]:
        ap[name] = nc.dram_tensor(name, shape, dt, kind="ExternalInput").ap()
    xTo = nc.dram_tensor("xTo", [D, TOKC], F32, kind="ExternalOutput").ap()
    with ExitStack() as es:
        pr = Prog(nc, es)
        cx = Ctx(nc, es, pr)
        v, s, g, t, sy = cx.v, cx.s, cx.g, cx.t, cx.sy
        sb = lambda n, sh, dt=F32: es.enter_context(nc.sbuf_tensor("po" + n, sh, dt))
        x_t = [sb(f"x{k}", [128, TOKC]) for k in range(8)]
        y_t = [sb(f"y{k}", [128, TOKC], BF16) for k in range(8)]
        bo = sb("bo", [128, 8])
        pp = PsumPipe(nc, es, "popp", [128, CH], nbanks=4)
        wload = Loader(nc, es, "powl", [128, 8 * 128], BF16, nbuf=3)
        pr.dma_start(bo[:], ap["bo_cm"])
        for k in range(8):
            pr.dma_start(x_t[k][:], ap["xT"][k * 128:(k + 1) * 128, :])
            pr.dma_start(y_t[k][:], ap["yTs"][k * 128:(k + 1) * 128, :])
        cx.finish_init()
        wo_r = ap["wo"].rearrange("(k p) c -> p k c", p=128)
        mmrec = {}
        for j in range(3):
            wload.issue(wo_r[:, :, j * 128:(j + 1) * 128])
        for c in range(TOKC // CH):
            cols = slice(c * CH, (c + 1) * CH)
            for m in range(8):
                key = (c, m)
                wb = wload.use(t)
                psum = pp.acquire()
                for k in range(8):
                    im = t.matmul(psum[:, :], wb[:, k * 128:(k + 1) * 128],
                                  y_t[k][:, cols], start=(k == 0), stop=(k == 7))
                pp.mm_done(im)
                mmrec[key] = pp.n_mm
                nidx = c * 8 + m + 3
                if nidx < 32:
                    nc2, nm2 = divmod(nidx, 8)
                    wload.issue(wo_r[:, :, nm2 * 128:(nm2 + 1) * 128],
                                wait=(pp.sem_mm, mmrec[key]))
                pt = pp.evac_wait(v)
                xsl = x_t[m][:, cols]
                ixa = v.tensor_scalar_add(xsl, xsl, bo[:, m:m + 1])
                pr.chain(ixa, v)
                ie = v.tensor_add(xsl, xsl, pt[:, :])
                pp.evac_done(ie)
        pp.drain(sy, t, v, s, g)
        for k in range(8):
            pr.dma_start(xTo[k * 128:(k + 1) * 128, :], x_t[k][:])
        pr.wait_dma(sy)
    return nc


def host_prep_attn0(inputs):
    wq = np.asarray(inputs["L0_attn_wq"], np.float32)
    wkv = np.asarray(inputs["L0_attn_wkv"], np.float32)
    wo = np.asarray(inputs["L0_attn_wo"], np.float32)
    bq = np.asarray(inputs["L0_attn_bq"], np.float32)
    bkv = np.asarray(inputs["L0_attn_bkv"], np.float32)
    bo = np.asarray(inputs["L0_attn_bo"], np.float32)
    qn = np.asarray(inputs["L0_attn_qn_w"], np.float32)
    kn = np.asarray(inputs["L0_attn_kn_w"], np.float32)
    pre = {
        "wq": wq, "wkv": wkv,
        "bq_cm": colmajor(bq), "bkv_cm": colmajor(bkv),
        "qn_cm": np.ascontiguousarray(np.tile(qn, 16).reshape(8, 128).T),
        "kn_cm": np.ascontiguousarray(np.tile(kn, 4).reshape(2, 128).T),
    }
    post = {"wo": wo, "bo_cm": colmajor(bo)}
    cores = []
    jj = np.arange(T, dtype=np.float64)
    tt = np.arange(T, dtype=np.float64)
    for hh in range(2):
        slopes = ALIBI_SLOPES[hh * 8:(hh + 1) * 8]
        ramps = np.concatenate([(sl * jj).astype(np.float32) for sl in slopes])
        basecol = np.empty((128, 8 * 32), np.float32)
        for h, sl in enumerate(slopes):
            bcm = (-sl * tt).astype(np.float32).reshape(32, 128).T
            basecol[:, h * 32:(h + 1) * 32] = bcm
        dmask = np.zeros((128, 4 * CH), np.float32)
        for r4 in range(4):
            p = np.arange(128)[:, None]
            cc = np.arange(CH)[None, :]
            dmask[:, r4 * CH:(r4 + 1) * CH] = np.where(cc > r4 * 128 + p, -1e30, 0.0)
        cores.append({"ramps": np.ascontiguousarray(ramps.reshape(8, T)),
                      "basecol": basecol, "dmask": dmask})
    return pre, post, cores


def build_ffn_l1(nc):
    from contextlib import ExitStack
    ap = {}
    for name, shape, dt in [
            ("xT", [D, TOKC], F32), ("wg", [D, DH], F32), ("wu", [D, DH], F32),
            ("wd", [DH, D], F32), ("wq", [D, 512], F32), ("det", [D, 4096], F32),
            ("ue", [4096, D], F32), ("keysT", [64, 512], F32),
            ("bg_cm", [128, 32], F32), ("bu_cm", [128, 32], F32),
            ("bd_cm", [128, 8], F32)]:
        ap[name] = nc.dram_tensor(name, shape, dt, kind="ExternalInput").ap()
    xTo = nc.dram_tensor("xTo", [D, TOKC], F32, kind="ExternalOutput").ap()

    NBLK = CH // 128
    MAGIC = 12582912.0

    with ExitStack() as es:
        pr = Prog(nc, es)
        cx = Ctx(nc, es, pr)
        v, s, g, t, sy = cx.v, cx.s, cx.g, cx.t, cx.sy
        sb = lambda n, sh, dt=F32: es.enter_context(nc.sbuf_tensor("f1" + n, sh, dt))
        x_t = [sb(f"x{k}", [128, TOKC]) for k in range(8)]
        xh_t = [sb(f"xh{k}", [128, CH], BF16) for k in range(8)]
        sq = sb("sq", [128, CH])
        row = sb("row", [1, 2 * CH])
        bc = sb("bc", [128, CH])
        qq = [sb(f"qq{k}", [128, CH], BF16) for k in range(4)]
        keysT = sb("keysT", [64, 512], BF16)
        iota64 = sb("iota64", [128, 64])
        iota4k = sb("iota4k", [128, 4096])
        i64rev = sb("i64rev", [128, 64])
        simt = sb("simt", [128, 64])
        qcur64 = sb("qcur64", [64, 128], BF16)
        pk8 = sb("pk8", [128, 8])
        sxv = sb("sxv", [128, 8 * 4])
        sxi = sb("sxi", [128, 8 * 4])
        syv = sb("syv", [128, 8 * 4])
        syi = sb("syi", [128, 8 * 4])
        asc = sb("asc", [128, 64])
        aix = sb("aix", [128, 64])
        sc8 = sb("sc8", [128, 8])
        ix8 = sb("ix8", [128, 8])
        w8 = sb("w8", [128, 8])
        sm8 = sb("sm8", [128, 8])
        t64 = sb("t64", [128, 64])
        t4k = sb("t4k", [128, 1024])
        allw = sb("allw", [128, 4096])
        selw = sb("selw", [128, 4096], BF16)
        selwtall = sb("selwtall", [128, 32 * 128], BF16)
        expacc = [sb(f"ea{k}", [128, CH]) for k in range(8)]
        gtmp = sb("gtmp", [128, CH], BF16)
        utmp = sb("utmp", [128, CH], BF16)
        hh = sb("hh", [128, 32 * CH], BF16)
        bias_g = sb("bg", [128, 32])
        bias_u = sb("bu", [128, 32])
        bias_d = sb("bd", [128, 8])

        pp = PsumPipe(nc, es, "f1pp", [128, CH], nbanks=3)
        psim = es.enter_context(nc.psum_tensor("f1psim", [128, 64], F32))
        ptr = es.enter_context(nc.psum_tensor("f1ptr", [128, 128], BF16))
        pexp = es.enter_context(nc.psum_tensor("f1pexp", [128, 128], F32))
        pln = es.enter_context(nc.psum_tensor("f1pln", [128, CH], F32))
        wload = Loader(nc, es, "f1wl", [128, 8 * 128], BF16, nbuf=2)
        dload = Loader(nc, es, "f1dl", [128, CH], BF16, nbuf=2)
        uload = Loader(nc, es, "f1ul", [128, 32 * 128], BF16, nbuf=1)

        iota_d = nc.inline_tensor(
            np.tile(np.arange(4096, dtype=np.float32), (128, 1)), "f1_iota4k")
        iota64_d = nc.inline_tensor(
            np.tile(np.arange(64, dtype=np.float32), (128, 1)), "f1_iota64")
        i64r_d = nc.inline_tensor(
            np.tile(63.0 - np.arange(64, dtype=np.float32), (128, 1)), "f1_i64r")
        pr.dma_start(iota4k[:], iota_d.ap())
        pr.dma_start(iota64[:], iota64_d.ap())
        pr.dma_start(i64rev[:], i64r_d.ap())
        pr.dma_start(keysT[:], ap["keysT"], eng=g)
        pr.dma_start(bias_g[:], ap["bg_cm"])
        pr.dma_start(bias_u[:], ap["bu_cm"])
        pr.dma_start(bias_d[:], ap["bd_cm"])
        for k in range(8):
            pr.dma_start(x_t[k][:], ap["xT"][k * 128:(k + 1) * 128, :])
        cx.finish_init()

        wq_r = ap["wq"].rearrange("(k p) c -> p k c", p=128)
        wg_r = ap["wg"].rearrange("(k p) c -> p k c", p=128)
        wu_r = ap["wu"].rearrange("(k p) c -> p k c", p=128)
        wd_r = ap["wd"].rearrange("(k p) c -> p k c", p=128)
        det_r = ap["det"].rearrange("(k p) c -> p k c", p=128)
        ue_r = ap["ue"].rearrange("(k p) c -> p k c", p=128)

        def top8_packed(src_ap, width, outv, outi, iot_rev, qscale):
            iv = v.tensor_scalar(out=t64[:, 0:width], in0=src_ap, scalar1=qscale,
                                 scalar2=MAGIC, op0=ALU.mult, op1=ALU.add)
            pr.chain(iv, v)
            iv = v.tensor_scalar(out=t64[:, 0:width], in0=t64[:, 0:width],
                                 scalar1=MAGIC, scalar2=64.0,
                                 op0=ALU.subtract, op1=ALU.mult)
            pr.chain(iv, v)
            iv = v.tensor_add(t64[:, 0:width], t64[:, 0:width], iot_rev)
            pr.chain(iv, v)
            iv = v.max(out=pk8[:], in_=t64[:, 0:width])
            pr.chain(iv, v)
            iv = v.tensor_scalar(out=outv, in0=pk8[:], scalar1=1.0 / 64.0,
                                 scalar2=MAGIC - 31.5 / 64.0,
                                 op0=ALU.mult, op1=ALU.add)
            pr.chain(iv, v)
            iv = v.tensor_scalar(out=outv, in0=outv, scalar1=MAGIC,
                                 scalar2=None, op0=ALU.subtract)
            pr.chain(iv, v)
            iv = v.scalar_tensor_tensor(out=outi, in0=outv, scalar=-64.0,
                                        in1=pk8[:], op0=ALU.mult, op1=ALU.add)
            pr.chain(iv, v)
            iv = v.tensor_scalar(out=outi, in0=outi, scalar1=-1.0, scalar2=63.0,
                                 op0=ALU.mult, op1=ALU.add)
            pr.chain(iv, v)
            iv = v.tensor_scalar(out=outv, in0=outv, scalar1=1.0 / qscale,
                                 scalar2=None, op0=ALU.mult)
            pr.chain(iv, v)
            return iv

        det_rec = []
        for mo in range(8):
            ivz = v.memset(expacc[mo][:], 0.0)
        pr.bar([ivz], [t, v])
        for c in range(TOKC // CH):
            cols = slice(c * CH, (c + 1) * CH)
            build_ln(cx, [xt[:, cols] for xt in x_t], [xx[:] for xx in xh_t],
                     sq, row, bc, pln)
            pr.sync([t, g], engines=("DVE",))
            for m in range(4):
                wload.issue(wq_r[:, :, m * 128:(m + 1) * 128],
                            wait=(pp.sem_mm, pp.n_mm - 1) if wload.n_issued >= 2 else None)
                wb = wload.use(t)
                psum = pp.acquire()
                for k in range(8):
                    im = t.matmul(psum[:, :], wb[:, k * 128:(k + 1) * 128],
                                  xh_t[k][:], start=(k == 0), stop=(k == 7))
                pp.mm_done(im)
                pt = pp.evac_wait(s)
                ie = s.activation(out=qq[m][:], in_=pt[:, :], func=AF.Identity)
                pp.evac_done(ie)
            pp.guard(t, v, s)
            for blk in range(NBLK):
                tb = slice(blk * 128, (blk + 1) * 128)
                for eo in range(8):
                    esl = slice(eo * CH, (eo + 1) * CH)
                    psum = pp.acquire()
                    for k in range(8):
                        wt = det_rec[dload.n_issued - 2] if dload.n_issued >= 2 else None
                        dload.issue(det_r[:, k, esl], wait=wt)
                        db = dload.use(t)
                        im = t.matmul(psum[:, :], xh_t[k][:, tb], db[:],
                                      start=(k == 0), stop=(k == 7))
                        if k < 7:
                            pr.bar([im], [])
                            det_rec.append((pr.esem["PE"], pr.ecnt["PE"]))
                        else:
                            pp.mm_done(im)
                            det_rec.append((pp.sem_mm, pp.n_mm))
                    pt = pp.evac_wait(v)
                    ie = v.tensor_copy(out=allw[:, esl], in_=pt[:, :])
                    pp.evac_done(ie)
                pp.guard(v, s, t)
                for ph in range(8):
                    p_, h_ = ph // 4, ph % 4
                    fidx = p_ * 4 + h_
                    qt = qq[fidx // 2]
                    rsel = (fidx % 2) * 64
                    pr.sync([sy], engines=("PE", "DVE"))
                    pp.guard(sy)
                    pr.dma_start(qcur64[:], qt[rsel:rsel + 64, tb])
                    pr.wait_dma(t)
                    imm = t.matmul(psim[:, :], qcur64[:],
                                   keysT[:, (h_ * 2 + p_) * 64:(h_ * 2 + p_ + 1) * 64],
                                   start=True, stop=True)
                    pr.bar([imm], [v])
                    iv = v.tensor_copy(out=simt[:, 0:64], in_=psim[:, :])
                    pr.chain(iv, v)
                    dst_v = (sxv if p_ == 0 else syv)[:, h_ * 8:(h_ + 1) * 8]
                    dst_i = (sxi if p_ == 0 else syi)[:, h_ * 8:(h_ + 1) * 8]
                    top8_packed(simt[:, 0:64], 64, dst_v, dst_i, i64rev[:, 0:64], 16384.0)
                for h_ in range(4):
                    for i8 in range(8):
                        iv = v.tensor_scalar_add(
                            asc[:, i8 * 8:(i8 + 1) * 8],
                            syv[:, h_ * 8:(h_ + 1) * 8],
                            sxv[:, h_ * 8 + i8:h_ * 8 + i8 + 1])
                        iv = v.scalar_tensor_tensor(
                            out=aix[:, i8 * 8:(i8 + 1) * 8],
                            in0=sxi[:, h_ * 8 + i8:h_ * 8 + i8 + 1].to_broadcast([128, 8]),
                            scalar=64.0, in1=syi[:, h_ * 8:(h_ + 1) * 8],
                            op0=ALU.mult, op1=ALU.add)
                    pr.chain(iv, v)
                    top8_packed(asc[:, 0:64], 64, sc8[:], pk8[:], i64rev[:, 0:64], 16384.0)
                    for j in range(8):
                        iv = v.scalar_tensor_tensor(out=t64[:, 0:64], in0=iota64[:, 0:64],
                                                    scalar=pk8[:, j:j + 1], in1=aix[:, 0:64],
                                                    op0=ALU.is_equal, op1=ALU.mult)
                        pr.chain(iv, v)
                        iv = v.reduce_sum(ix8[:, j:j + 1], t64[:, 0:64], axis=AX.X)
                        pr.chain(iv, v)
                    iv = v.reduce_max(sm8[:, 0:1], sc8[:], axis=AX.X)
                    pr.chain(iv, v)
                    iv = v.tensor_scalar_mul(sm8[:, 0:1], sm8[:, 0:1], -1.0)
                    pr.bar([iv], [s])
                    ie = s.activation(out=sm8[:], in_=sc8[:], func=AF.Exp,
                                      bias=sm8[:, 0:1])
                    pr.bar([ie], [v])
                    iv = v.reduce_sum(w8[:, 0:1], sm8[:], axis=AX.X)
                    pr.chain(iv, v)
                    iv = v.reciprocal(w8[:, 0:1], w8[:, 0:1])
                    pr.chain(iv, v)
                    iv = v.tensor_scalar(out=sm8[:], in0=sm8[:], scalar1=w8[:, 0:1],
                                         scalar2=None, op0=ALU.mult)
                    pr.chain(iv, v)
                    for j in range(8):
                        for qr in range(4):
                            qs = slice(qr * 1024, (qr + 1) * 1024)
                            iv = v.scalar_tensor_tensor(
                                out=t4k[:], in0=iota4k[:, qs],
                                scalar=ix8[:, j:j + 1], in1=allw[:, qs],
                                op0=ALU.is_equal, op1=ALU.mult)
                            pr.chain(iv, v)
                            iv = v.reduce_sum(w8[:, j:j + 1] if qr == 0 else sm8[:, 0:1],
                                              t4k[:], axis=AX.X)
                            pr.chain(iv, v)
                            if qr > 0:
                                iv = v.tensor_add(w8[:, j:j + 1], w8[:, j:j + 1],
                                                  sm8[:, 0:1])
                                pr.chain(iv, v)
                    pr.sync([s], engines=("DVE",))
                    ie = s.activation(out=w8[:], in_=w8[:], func=AF.Silu)
                    pr.bar([ie], [v])
                    iv = v.tensor_mul(sm8[:], sm8[:], w8[:])
                    pr.chain(iv, v)
                    if h_ == 0:
                        iv0 = v.memset(selw[:], 0.0)
                        pr.chain(iv0, v)
                    for j in range(8):
                        for qr in range(4):
                            qs = slice(qr * 1024, (qr + 1) * 1024)
                            iv = v.scalar_tensor_tensor(
                                out=t4k[:], in0=iota4k[:, qs],
                                scalar=ix8[:, j:j + 1],
                                in1=sm8[:, j:j + 1].to_broadcast([128, 1024]),
                                op0=ALU.is_equal, op1=ALU.mult)
                            pr.chain(iv, v)
                            iv = v.tensor_add(selw[:, qs], selw[:, qs], t4k[:])
                            pr.chain(iv, v)
                pr.sync([t], engines=("DVE",))
                for et in range(32):
                    imt = t.transpose(ptr[:], selw[:, et * 128:(et + 1) * 128],
                                      cx.ident_bf[:])
                    pr.bar([imt], [v])
                    iev = v.tensor_copy(out=selwtall[:, et * 128:(et + 1) * 128],
                                        in_=ptr[:])
                    pr.bar([iev], [t])
                for mo in range(8):
                    uload.issue(ue_r[:, :, mo * 128:(mo + 1) * 128],
                                wait=(pr.esem["PE"], pr.ecnt["PE"]))
                    ub = uload.use(t)
                    for et in range(32):
                        im = t.matmul(pexp[:, :], ub[:, et * 128:(et + 1) * 128],
                                      selwtall[:, et * 128:(et + 1) * 128],
                                      start=(et == 0), stop=(et == 31))
                    pr.bar([im], [v])
                    iev = v.tensor_add(expacc[mo][:, blk * 128:(blk + 1) * 128],
                                       expacc[mo][:, blk * 128:(blk + 1) * 128],
                                       pexp[:, 0:128])
                    pr.bar([iev], [t])
            for m in range(32):
                wload.issue(wg_r[:, :, m * 128:(m + 1) * 128],
                            wait=(pp.sem_mm, pp.n_mm - 1) if wload.n_issued >= 2 else None)
                wb = wload.use(t)
                psum = pp.acquire()
                for k in range(8):
                    im = t.matmul(psum[:, :], wb[:, k * 128:(k + 1) * 128],
                                  xh_t[k][:], start=(k == 0), stop=(k == 7))
                pp.mm_done(im)
                pt = pp.evac_wait(s)
                ie = s.activation(out=gtmp[:], in_=pt[:, :], func=AF.Silu,
                                  bias=bias_g[:, m:m + 1])
                pp.evac_done(ie)
                wload.issue(wu_r[:, :, m * 128:(m + 1) * 128],
                            wait=(pp.sem_mm, pp.n_mm - 1) if wload.n_issued >= 2 else None)
                wb = wload.use(t)
                psum = pp.acquire()
                for k in range(8):
                    im = t.matmul(psum[:, :], wb[:, k * 128:(k + 1) * 128],
                                  xh_t[k][:], start=(k == 0), stop=(k == 7))
                pp.mm_done(im)
                pt = pp.evac_wait(s)
                ie2 = s.activation(out=utmp[:], in_=pt[:, :], func=AF.Identity,
                                   bias=bias_u[:, m:m + 1])
                pp.evac_done(ie2)
                pp.guard(v)
                iv = v.tensor_mul(hh[:, m * CH:(m + 1) * CH], gtmp[:], utmp[:])
                pr.bar([iv], [t, s])
            for mo in range(8):
                uload.issue(wd_r[:, :, mo * 128:(mo + 1) * 128],
                            wait=(pp.sem_mm, pp.n_mm) if mo else (pr.esem["PE"], pr.ecnt["PE"]))
                wb = uload.use(t)
                psum = pp.acquire()
                for kk in range(32):
                    im = t.matmul(psum[:, :], wb[:, kk * 128:(kk + 1) * 128],
                                  hh[:, kk * CH:(kk + 1) * CH],
                                  start=(kk == 0), stop=(kk == 31))
                pp.mm_done(im)
                pt = pp.evac_wait(v)
                pp.guard(v)
                xsl = x_t[mo][:, cols]
                iv = v.tensor_add(sq[:], pt[:, :], expacc[mo][:, :])
                pr.chain(iv, v)
                iv = v.tensor_scalar_add(sq[:], sq[:], bias_d[:, mo:mo + 1])
                pr.chain(iv, v)
                ie = v.scalar_tensor_tensor(out=xsl, in0=sq[:], scalar=0.5, in1=xsl,
                                            op0=ALU.mult, op1=ALU.add)
                pp.evac_done(ie)
            pp.drain(t, v, s, g, sy)
            for mo in range(8):
                iv = v.memset(expacc[mo][:], 0.0)
            pr.bar([iv], [t, v])
        pr.sync([sy])
        for k in range(8):
            pr.dma_start(xTo[k * 128:(k + 1) * 128, :], x_t[k][:])
        pr.wait_dma(sy)
    return nc


def host_prep_ffn_l1(inputs):
    keys = np.asarray(inputs["L1_ffn_keys"], np.float32)
    keysT = np.ascontiguousarray(np.transpose(keys, (3, 0, 2, 1)).reshape(64, 512))
    return {
        "wg": np.asarray(inputs["L1_ffn_Wg"], np.float32),
        "wu": np.asarray(inputs["L1_ffn_Wu"], np.float32),
        "wd": np.asarray(inputs["L1_ffn_Wd"], np.float32),
        "wq": np.asarray(inputs["L1_ffn_Wq"], np.float32),
        "det": np.ascontiguousarray(np.asarray(inputs["L1_ffn_down_embed"], np.float32).T),
        "ue": np.asarray(inputs["L1_ffn_up_embed"], np.float32),
        "keysT": keysT,
        "bg_cm": colmajor(np.asarray(inputs["L1_ffn_bg"], np.float32)),
        "bu_cm": colmajor(np.asarray(inputs["L1_ffn_bu"], np.float32)),
        "bd_cm": colmajor(np.asarray(inputs["L1_ffn_bd"], np.float32)),
    }


DIN = 2048
DST = 16
DBLK = DIN // NCORES


def build_mamba_proj(nc):
    from contextlib import ExitStack
    ap = {}
    for name, shape, dt in [("xT", [D, TOKC], F32), ("winT", [D, 2 * DIN], F32)]:
        ap[name] = nc.dram_tensor(name, shape, dt, kind="ExternalInput").ap()
    xz_o = nc.dram_tensor("xzT", [2 * DIN, TOKC], F32, kind="ExternalOutput").ap()
    with ExitStack() as es:
        pr = Prog(nc, es)
        cx = Ctx(nc, es, pr)
        v, s, g, t, sy = cx.v, cx.s, cx.g, cx.t, cx.sy
        sb = lambda n, sh, dt=F32: es.enter_context(nc.sbuf_tensor("mp" + n, sh, dt))
        x_t = [sb(f"x{k}", [128, TOKC]) for k in range(8)]
        xh_t = [sb(f"xh{k}", [128, CH], BF16) for k in range(8)]
        sq = sb("sq", [128, CH])
        row = sb("row", [1, 2 * CH])
        bc = sb("bc", [128, CH])
        ot = sb("ot", [128, CH])
        pln = es.enter_context(nc.psum_tensor("mppln", [128, CH], F32))
        pp = PsumPipe(nc, es, "mppp", [128, CH], nbanks=4)
        wload = Loader(nc, es, "mpwl", [128, 8 * 128], BF16, nbuf=3)
        for k in range(8):
            pr.dma_start(x_t[k][:], ap["xT"][k * 128:(k + 1) * 128, :])
        cx.finish_init()
        win_r = ap["winT"].rearrange("(k p) c -> p k c", p=128)
        mmrec = {}
        for c in range(TOKC // CH):
            cols = slice(c * CH, (c + 1) * CH)
            build_ln(cx, [xt[:, cols] for xt in x_t], [xx[:] for xx in xh_t],
                     sq, row, bc, pln)
            pr.sync([t, g], engines=("DVE",))
            if c == 0:
                for j in range(3):
                    wload.issue(win_r[:, :, j * 128:(j + 1) * 128])
            for m in range(32):
                wb = wload.use(t)
                psum = pp.acquire()
                for k in range(8):
                    im = t.matmul(psum[:, :], wb[:, k * 128:(k + 1) * 128],
                                  xh_t[k][:], start=(k == 0), stop=(k == 7))
                pp.mm_done(im)
                mmrec[(c, m)] = pp.n_mm
                nidx = c * 32 + m + 3
                if nidx < 4 * 32:
                    nm = nidx % 32
                    wload.issue(win_r[:, :, nm * 128:(nm + 1) * 128],
                                wait=(pp.sem_mm, pp.n_mm))
                pt = pp.evac_wait(v)
                iv = v.tensor_copy(out=ot[:], in_=pt[:, :])
                pp.evac_done(iv)
                pp.guard(sy)
                pr.dma_start(xz_o[m * 128:(m + 1) * 128, cols], ot[:])
                pr.wait_dma(v)
        pr.wait_dma(sy)
    return nc


def build_mamba_conv(nc):
    from contextlib import ExitStack
    ap = {}
    for name, shape, dt in [("xiT", [DBLK, NTOK], F32), ("cw_cm", [128, 2, 4], F32),
                            ("cb_cm", [128, 2], F32)]:
        ap[name] = nc.dram_tensor(name, shape, dt, kind="ExternalInput").ap()
    u_o = nc.dram_tensor("uT", [DBLK, NTOK], F32, kind="ExternalOutput").ap()
    with ExitStack() as es:
        pr = Prog(nc, es)
        cx = Ctx(nc, es, pr)
        v, s, g, t, sy = cx.v, cx.s, cx.g, cx.t, cx.sy
        sb = lambda n, sh, dt=F32: es.enter_context(nc.sbuf_tensor("mc" + n, sh, dt))
        xp = sb("xp", [128, T + 3])
        acc = sb("acc", [128, T])
        uo = sb("uo", [128, T])
        cw = sb("cw", [128, 2 * 4])
        cb = sb("cb", [128, 2])
        pr.dma_start(cw[:], ap["cw_cm"])
        pr.dma_start(cb[:], ap["cb_cm"])
        cx.finish_init()
        for b in range(B):
            for dtile in range(2):
                rows = slice(dtile * 128, (dtile + 1) * 128)
                tcols = slice(b * T, (b + 1) * T)
                iz = v.memset(xp[:, 0:3], 0.0)
                pr.chain(iz, v)
                pr.sync([sy], engines=("DVE", "Activation"))
                pr.dma_start(xp[:, 3:T + 3], ap["xiT"][rows, tcols])
                pr.wait_dma(v)
                iv = v.tensor_scalar(out=acc[:], in0=xp[:, 0:T],
                                     scalar1=cw[:, dtile * 4:dtile * 4 + 1],
                                     scalar2=None, op0=ALU.mult)
                pr.chain(iv, v)
                for tau in range(1, 4):
                    iv = v.scalar_tensor_tensor(
                        out=acc[:], in0=xp[:, tau:T + tau],
                        scalar=cw[:, dtile * 4 + tau:dtile * 4 + tau + 1],
                        in1=acc[:], op0=ALU.mult, op1=ALU.add)
                    pr.chain(iv, v)
                pr.sync([s], engines=("DVE",))
                ie = s.activation(out=uo[:], in_=acc[:], func=AF.Silu,
                                  bias=cb[:, dtile:dtile + 1])
                pr.bar([ie], [sy])
                pr.dma_start(u_o[rows, tcols], uo[:])
                pr.wait_dma(v, s)
        pr.wait_dma(sy)
    return nc


def build_mamba_scan(nc):
    from contextlib import ExitStack
    ap = {}
    for name, shape, dt in [
            ("uF", [DIN, NTOK], F32),
            ("uL", [DBLK, NTOK], F32),
            ("zL", [DBLK, NTOK], F32),
            ("wxT", [DIN, 32], F32),
            ("wdtT", [DIN, DBLK], F32),
            ("bdt_cm", [128, 2], F32), ("alog_cm", [128, 2 * DST], F32),
            ("dD_cm", [128, 2], F32)]:
        ap[name] = nc.dram_tensor(name, shape, dt, kind="ExternalInput").ap()
    y_o = nc.dram_tensor("yT", [DBLK, NTOK], BF16, kind="ExternalOutput").ap()
    NTC = T // CH
    with ExitStack() as es:
        pr = Prog(nc, es)
        cx = Ctx(nc, es, pr)
        v, s, g, t, sy = cx.v, cx.s, cx.g, cx.t, cx.sy
        sb = lambda n, sh, dt=F32: es.enter_context(nc.sbuf_tensor("ms" + n, sh, dt))
        dt_b = [sb(f"dt{i}", [128, T]) for i in range(2)]
        dtu = [sb(f"dtu{i}", [128, T]) for i in range(2)]
        u_b = [sb(f"u{i}", [128, T]) for i in range(2)]
        z_b = [sb(f"z{i}", [128, T]) for i in range(2)]
        yac = sb("yac", [128, T])
        sA = sb("sA", [128, T])
        dAc = sb("dAc", [128, CH])
        dBc = sb("dBc", [128, CH])
        tmpc = sb("tmpc", [128, CH])
        bcb = sb("bcb", [32, T], BF16)
        wx = sb("wx", [128, 16 * 32], BF16)
        wdt_sb = sb("wdt", [128, 16 * 256], BF16)
        bdt = sb("bdt", [128, 2])
        alog = sb("alog", [128, 2 * DST])
        dD = sb("dD", [128, 2])
        sel32 = sb("sel32", [32, 32 * 128], BF16)
        negone = sb("negone", [128, 1])
        pp = PsumPipe(nc, es, "mspp", [128, CH], nbanks=3)
        pbc = es.enter_context(nc.psum_tensor("mspbc", [32, CH], F32))
        pbb = es.enter_context(nc.psum_tensor("mspbb", [128, CH], F32))
        pcc = es.enter_context(nc.psum_tensor("mspcc", [128, CH], F32))
        uload = Loader(nc, es, "msul", [128, CH], BF16, nbuf=2)

        sel_np = np.zeros((32, 32 * 128), np.float32)
        for n_ in range(32):
            sel_np[n_, n_ * 128:(n_ + 1) * 128] = 1.0
        sel_d = nc.inline_tensor(sel_np.astype(BF16NP), "ms_sel32")
        pr.dma_start(sel32[:], sel_d.ap())
        pr.dma_start(bdt[:], ap["bdt_cm"])
        pr.dma_start(alog[:], ap["alog_cm"])
        pr.dma_start(dD[:], ap["dD_cm"])
        wx_r = ap["wxT"].rearrange("(k p) c -> p k c", p=128)
        for k in range(16):
            pr.dma_start(wx[:, k * 32:(k + 1) * 32], wx_r[:, k, :], eng=g)
        wdt_r2 = ap["wdtT"].rearrange("(k p) c -> p k c", p=128)
        pr.dma_start(wdt_sb.rearrange("p (k c) -> p k c", k=16), wdt_r2, eng=g)
        cx.finish_init()
        iz = v.memset(negone[:], -1.0)
        pr.bar([iz], [s])
        ie = s.activation(out=alog[:], in_=alog[:], func=AF.Exp)
        pr.bar([ie], [v])
        iv = v.tensor_scalar_mul(alog[:], alog[:], -1.0)
        pr.bar([iv], [s, t])

        for b in range(B):
            tcols = slice(b * T, (b + 1) * T)
            pr.sync([sy])
            for i in range(2):
                rows = slice(i * 128, (i + 1) * 128)
                pr.dma_start(u_b[i][:], ap["uL"][rows, tcols])
                pr.dma_start(z_b[i][:], ap["zL"][rows, tcols])
            pr.wait_dma(v, s, t)
            for tc in range(NTC):
                csl = slice(tc * CH, (tc + 1) * CH)
                for k in range(16):
                    uload.issue(ap["uF"][k * 128:(k + 1) * 128,
                                         b * T + tc * CH: b * T + (tc + 1) * CH],
                                wait=(pr.esem["PE"], pr.ecnt["PE"] - 1)
                                if uload.n_issued >= 2 else None)
                    ub = uload.use(t)
                    t.matmul(pbc[:, :], wx[:, k * 32:(k + 1) * 32], ub[:],
                             start=(k == 0), stop=(k == 15))
                    for i in range(2):
                        psel = pp.tiles[i]
                        im = t.matmul(psel[:, :],
                                      wdt_sb[:, k * 256 + i * 128:k * 256 + (i + 1) * 128],
                                      ub[:], start=(k == 0), stop=(k == 15))
                    pr.bar([im], [])
                pr.sync([v, s], engines=("PE",))
                iv = v.tensor_copy(out=bcb[:, csl], in_=pbc[:, :])
                pr.bar([iv], [v, t])
                for i in range(2):
                    ie = s.activation(out=dt_b[i][:, csl], in_=pp.tiles[i][:, :],
                                      func=AF.Identity, bias=bdt[:, i:i + 1])
                    pr.bar([ie], [t, v, s])
            for i in range(2):
                ia = s.activation(out=sA[:], in_=dt_b[i][:], func=AF.Abs)
                pr.bar([ia], [s])
                ia = s.activation(out=sA[:], in_=sA[:], func=AF.Exp, scale=negone[:])
                pr.bar([ia], [s])
                ia = s.activation(out=sA[:], in_=sA[:], func=AF.Ln, bias=1.0)
                pr.bar([ia], [s])
                ir = s.activation(out=dt_b[i][:], in_=dt_b[i][:], func=AF.Relu)
                pr.bar([ir], [v])
                iv = v.tensor_add(dt_b[i][:], dt_b[i][:], sA[:])
                pr.chain(iv, v)
                iv = v.tensor_mul(dtu[i][:], dt_b[i][:], u_b[i][:])
                pr.bar([iv], [s, t])
            for i in range(2):
                ivy = v.memset(yac[:], 0.0)
                pr.bar([ivy], [v, s, t])
                for n_ in range(DST):
                    for tc in range(NTC):
                        csl = slice(tc * CH, (tc + 1) * CH)
                        imb = t.matmul(pbb[:, :], sel32[:, n_ * 128:(n_ + 1) * 128],
                                       bcb[:, csl], start=True, stop=True)
                        imc = t.matmul(pcc[:, :], sel32[:, (n_ + DST) * 128:(n_ + DST + 1) * 128],
                                       bcb[:, csl], start=True, stop=True)
                        pr.bar([imc], [s, v])
                        ie = s.activation(out=dAc[:], in_=dt_b[i][:, csl], func=AF.Exp,
                                          scale=alog[:, i * DST + n_:i * DST + n_ + 1])
                        pr.bar([ie], [v])
                        iv = v.tensor_mul(dBc[:], dtu[i][:, csl], pbb[:, :])
                        pr.chain(iv, v)
                        ini = 0.0 if tc == 0 else sA[:, tc * CH - 1:tc * CH]
                        iv = v.tensor_tensor_scan(out=sA[:, csl], data0=dAc[:],
                                                  data1=dBc[:], initial=ini,
                                                  op0=ALU.mult, op1=ALU.add)
                        pr.chain(iv, v)
                        iv = v.tensor_mul(tmpc[:], sA[:, csl], pcc[:, :])
                        pr.chain(iv, v)
                        iv = v.tensor_add(yac[:, csl], yac[:, csl], tmpc[:])
                        pr.bar([iv], [t, v, s])
                ie = s.activation(out=z_b[i][:], in_=z_b[i][:], func=AF.Silu)
                pr.bar([ie], [v])
                iv = v.scalar_tensor_tensor(out=yac[:], in0=u_b[i][:],
                                            scalar=dD[:, i:i + 1], in1=yac[:],
                                            op0=ALU.mult, op1=ALU.add)
                pr.chain(iv, v)
                iv = v.tensor_mul(sA[:], yac[:], z_b[i][:])
                pr.bar([iv], [g])
                pr.dma_start(y_o[i * 128:(i + 1) * 128, tcols], sA[:, 0:T], eng=g)
                pr.wait_dma(v, s, t)
        pr.wait_dma(sy)
    return nc


def build_mamba_out(nc):
    from contextlib import ExitStack
    ap = {}
    for name, shape, dt in [("xT", [D, TOKC], F32), ("yTs", [DIN, TOKC], BF16),
                            ("woutT", [DIN, D], F32)]:
        ap[name] = nc.dram_tensor(name, shape, dt, kind="ExternalInput").ap()
    xTo = nc.dram_tensor("xTo", [D, TOKC], F32, kind="ExternalOutput").ap()
    with ExitStack() as es:
        pr = Prog(nc, es)
        cx = Ctx(nc, es, pr)
        v, s, g, t, sy = cx.v, cx.s, cx.g, cx.t, cx.sy
        sb = lambda n, sh, dt=F32: es.enter_context(nc.sbuf_tensor("mo" + n, sh, dt))
        x_t = [sb(f"x{k}", [128, TOKC]) for k in range(8)]
        y_t = [sb(f"y{k}", [128, TOKC], BF16) for k in range(16)]
        pp = PsumPipe(nc, es, "mopp", [128, CH], nbanks=4)
        wload = Loader(nc, es, "mowl", [128, 16 * 128], BF16, nbuf=2)
        for k in range(8):
            pr.dma_start(x_t[k][:], ap["xT"][k * 128:(k + 1) * 128, :])
        for k in range(16):
            pr.dma_start(y_t[k][:], ap["yTs"][k * 128:(k + 1) * 128, :])
        cx.finish_init()
        wo_r = ap["woutT"].rearrange("(k p) c -> p k c", p=128)
        for c in range(TOKC // CH):
            cols = slice(c * CH, (c + 1) * CH)
            for m in range(8):
                wload.issue(wo_r[:, :, m * 128:(m + 1) * 128],
                            wait=(pp.sem_mm, pp.n_mm - 1) if wload.n_issued >= 2 else None)
                wb = wload.use(t)
                psum = pp.acquire()
                for k in range(16):
                    im = t.matmul(psum[:, :], wb[:, k * 128:(k + 1) * 128],
                                  y_t[k][:, cols], start=(k == 0), stop=(k == 15))
                pp.mm_done(im)
                pt = pp.evac_wait(v)
                pp.guard(v)
                xsl = x_t[m][:, cols]
                ie = v.tensor_add(xsl, xsl, pt[:, :])
                pp.evac_done(ie)
        pp.drain(sy, t, v, s, g)
        for k in range(8):
            pr.dma_start(xTo[k * 128:(k + 1) * 128, :], x_t[k][:])
        pr.wait_dma(sy)
    return nc


def host_prep_mamba(inputs):
    W_in = np.asarray(inputs["L1_attn_W_in"], np.float32)
    conv_w = np.asarray(inputs["L1_attn_conv_w"], np.float32)
    conv_b = np.asarray(inputs["L1_attn_conv_b"], np.float32)
    W_x = np.asarray(inputs["L1_attn_W_x"], np.float32)
    W_dt = np.asarray(inputs["L1_attn_W_dt"], np.float32)
    b_dt = np.asarray(inputs["L1_attn_b_dt"], np.float32)
    A_log = np.asarray(inputs["L1_attn_A_log"], np.float32)
    Dv = np.asarray(inputs["L1_attn_D"], np.float32)
    W_out = np.asarray(inputs["L1_attn_W_out"], np.float32)
    proj = {"winT": np.ascontiguousarray(W_in.T)}
    convs, scans = [], []
    for c in range(NCORES):
        rows = slice(c * DBLK, (c + 1) * DBLK)
        cw = conv_w[rows, 0, :]
        cw_cm = np.ascontiguousarray(cw.reshape(2, 128, 4).transpose(1, 0, 2))
        convs.append({"cw_cm": cw_cm,
                      "cb_cm": colmajor(conv_b[rows])})
        al = A_log[rows]
        al_cm = np.ascontiguousarray(al.reshape(2, 128, DST).transpose(1, 0, 2)
                                     ).reshape(128, 2 * DST)
        scans.append({
            "wxT": np.ascontiguousarray(W_x.T),
            "wdtT": np.ascontiguousarray(W_dt.T[:, rows]),
            "bdt_cm": colmajor(b_dt[rows]),
            "alog_cm": al_cm,
            "dD_cm": colmajor(Dv[rows]),
        })
    out = {"woutT": np.ascontiguousarray(W_out.T)}
    return proj, convs, scans, out


def _run(nc_builder, per_core_fn, shared, out_names):
    nc = bass.Bass("TRN2", target_bir_lowering=False)
    nc_builder(nc)
    in_maps = []
    for c in range(NCORES):
        m = dict(shared) if not callable(shared) else dict(shared(c))
        m.update(per_core_fn(c))
        in_maps.append(m)
    res = run_bass_kernel_spmd(nc, in_maps, core_ids=list(range(NCORES)))
    return [[np.asarray(r[name]) for r in res.results] for name in out_names]


def kernel(**inputs):
    inputs = {k: np.asarray(v) for k, v in inputs.items()}
    x = inputs["L0_attn_x"]
    xT = np.ascontiguousarray(x.reshape(NTOK, D).T).astype(np.float32)

    sl = lambda a, c: np.ascontiguousarray(a[:, c * TOKC:(c + 1) * TOKC])

    f0_shared = host_prep_ffn_l0(inputs, "L0")
    a0_pre, a0_post, a0_cores = host_prep_attn0(inputs)
    f1_shared = host_prep_ffn_l1(inputs)
    mproj, mconvs, mscans, mout = host_prep_mamba(inputs)

    def ffn0(xT):
        (outs,) = _run(build_ffn_l0, lambda c: {"xT": sl(xT, c)}, f0_shared, ["xTo"])
        return np.concatenate(outs, axis=1)

    def attn0(xT):
        qh_l, kh_l, vh_l = _run(build_attn0_pre, lambda c: {"xT": sl(xT, c)},
                                a0_pre, ["qh", "kh", "vh"])
        qh = np.concatenate(qh_l, axis=1)
        kh = np.concatenate(kh_l, axis=1)
        vh = np.concatenate(vh_l, axis=1)

        def core_in(c):
            b, hh = c // 2, c % 2
            tsl = slice(b * T, (b + 1) * T)
            return {"qh": np.ascontiguousarray(qh[hh * 512:(hh + 1) * 512, tsl]),
                    "kh": np.ascontiguousarray(kh[hh * 128:(hh + 1) * 128, tsl]),
                    "vh": np.ascontiguousarray(vh[hh * 128:(hh + 1) * 128, tsl])}
        (yT_l,) = _run(build_attn0_core, core_in,
                       lambda c: a0_cores[c % 2], ["yT"])
        yT = np.zeros((1024, NTOK), dtype=yT_l[0].dtype)
        for c in range(NCORES):
            b, hh = c // 2, c % 2
            yT[hh * 512:(hh + 1) * 512, b * T:(b + 1) * T] = yT_l[c]
        (outs,) = _run(build_attn0_post,
                       lambda c: {"xT": sl(xT, c), "yTs": sl(yT, c)},
                       a0_post, ["xTo"])
        return np.concatenate(outs, axis=1)

    def ffn1(xT):
        (outs,) = _run(build_ffn_l1, lambda c: {"xT": sl(xT, c)}, f1_shared, ["xTo"])
        return np.concatenate(outs, axis=1)

    def mamba(xT):
        (xz_l,) = _run(build_mamba_proj, lambda c: {"xT": sl(xT, c)}, mproj, ["xzT"])
        xz = np.concatenate(xz_l, axis=1)
        (u_l,) = _run(build_mamba_conv,
                      lambda c: {"xiT": np.ascontiguousarray(
                          xz[c * DBLK:(c + 1) * DBLK, :])},
                      lambda c: mconvs[c], ["uT"])
        uF = np.concatenate(u_l, axis=0)
        (y_l,) = _run(build_mamba_scan,
                      lambda c: {"uF": uF,
                                 "uL": np.ascontiguousarray(uF[c * DBLK:(c + 1) * DBLK, :]),
                                 "zL": np.ascontiguousarray(
                                     xz[DIN + c * DBLK:DIN + (c + 1) * DBLK, :])},
                      lambda c: mscans[c], ["yT"])
        yF = np.concatenate(y_l, axis=0)
        (outs,) = _run(build_mamba_out,
                       lambda c: {"xT": sl(xT, c), "yTs": sl(yF, c)},
                       mout, ["xTo"])
        return np.concatenate(outs, axis=1)

    xT = ffn0(xT)
    xT = attn0(xT)
    xT = ffn0(xT)
    xT = ffn1(xT)
    xT = mamba(xT)
    xT = ffn1(xT)
    return np.ascontiguousarray(xT.T.reshape(B, T, D)).astype(np.float32)
